# version 22
# baseline (speedup 1.0000x reference)
import numpy as np
import ml_dtypes

import concourse.bass as bass
import concourse.mybir as mybir
from concourse.bass_utils import run_bass_kernel_spmd

F32 = mybir.dt.float32
BF16 = mybir.dt.bfloat16
I32 = mybir.dt.int32
AF = mybir.ActivationFunctionType
ALU = mybir.AluOpType
AX = mybir.AxisListType

NCORES = 8
TOK = 1024
NH = 2
NTT = 4
D = 64
EPS = 1e-5
INV_D = 1.0 / 2048.0

SYNC, PE, VEC, SCA, POOL = "sync", "tensor", "vector", "scalar", "gpsimd"


def build_program():
    nc = bass.Bass("TRN2", target_bir_lowering=False, debug=False, detect_race_conditions=False)

    d_x = nc.dram_tensor("x", [TOK, 2048], F32, kind="ExternalInput").ap()
    d_y = nc.dram_tensor("y", [TOK, 2048], F32, kind="ExternalOutput").ap()
    d_wqkv = nc.dram_tensor("wqkv_p", [128, 3072], F32, kind="ExternalInput").ap()
    d_rows = nc.dram_tensor("rows_f32", [1, 6400], F32, kind="ExternalInput").ap()
    d_wout = nc.dram_tensor("wout_bd", [128, 128], F32, kind="ExternalInput").ap()
    d_ident = nc.dram_tensor("identity", [128, 128], BF16, kind="ExternalInput").ap()
    d_scales = nc.dram_tensor("scales", [128, 144], F32, kind="ExternalInput").ap()
    d_wg = nc.dram_tensor("wg_s", [64, 128, 2048], I32, kind="ExternalInput").ap()
    d_wu = nc.dram_tensor("wu_s", [64, 128, 2048], I32, kind="ExternalInput").ap()
    d_wd = nc.dram_tensor("wd_s", [16, 128, 8192], I32, kind="ExternalInput").ap()
    import os as _os0
    _r1kind = "ExternalOutput" if _os0.environ.get("KR1") else "Internal"
    d_r1 = nc.dram_tensor("r1_scratch", [TOK, 2048], F32, kind=_r1kind).ap()

    A = lambda name, shape, dt: nc.alloc_sbuf_tensor(name, shape, dt)

    wqkv_sb = A("wqkv_sb", [128, 3072], BF16).ap()
    woutbd = A("woutbd", [128, 128], BF16).ap()
    ident = A("ident_sb", [128, 128], BF16).ap()
    scales = A("scales_sb", [128, 208], F32).ap()
    rows = A("rows_sb", [1, 6400], BF16).ap()
    stats = A("stats_sb", [128, 16], F32).ap()
    cns = A("cns_sb", [128, 6], F32).ap()
    onesc = A("onesc_sb", [128, 1], BF16).ap()
    ones_fr = A("ones_fr_sb", [1, 128], F32).ap()
    crow = A("crow_sb", [1, 512], BF16).ap()
    crow_f = A("crow_f_sb", [1, 512], F32).ap()
    zf = A("zf_sb", [128, 256], F32).ap()

    rsl = [A(f"rslot{k}", [128, 2048], F32).ap() for k in range(2)]
    tm_scr = A("tm_scr", [128, 2048], BF16).ap()
    xf = [A(f"xf{p}", [128, 16 * 512], BF16).ap() for p in range(2)]
    scr1 = A("scr1", [128, 1024], BF16).ap()
    scr2 = A("scr2", [128, 1024], BF16).ap()
    dsb = A("dsb", [128, 2 * 512], BF16).ap()
    wstage = A("wstage", [128, 8192], BF16).ap()

    qkvB_base = nc.sbuf_base
    qkvB = A("qkvB", [128, 6144], BF16).ap()
    ebB_base = nc.sbuf_base
    ebA2 = A("ebA2", [128, 8192], BF16).ap()
    ebB0 = nc.alloc_sbuf_tensor_at("ebB0", [128, 4096], BF16, offset=ebB_base).ap()
    ebB1 = nc.alloc_sbuf_tensor_at("ebB1", [128, 4096], BF16, offset=ebB_base + 8192).ap()
    zscrB = A("zscrB", [128, 2048], BF16).ap()
    ctx_f = A("ctx_f", [128, 2048], F32).ap()
    jit1B = A("jit1B", [128, 2048], BF16).ap()
    attnoutB = nc.alloc_sbuf_tensor_at("attnoutB", [128, 2048], BF16, offset=qkvB_base).ap()
    ctxTB = nc.alloc_sbuf_tensor_at("ctxTB", [128, 2048], BF16, offset=qkvB_base + 4096).ap()

    h_base = nc.sbuf_base
    h_sb = A("h_sb", [128, 64 * 512], BF16).ap()

    def at(name, off_bf16, shape, dt):
        return nc.alloc_sbuf_tensor_at(name, shape, dt, offset=h_base + 2 * off_bf16).ap()

    qkvA = at("qkvA", 0, [128, 6144], BF16)
    ebA0 = at("ebA0", 6144, [128, 8192], BF16)
    ebA1 = at("ebA1", 6144 + 8192, [128, 8192], BF16)
    zscrA = at("zscrA", 22528, [128, 4096], BF16)
    jit1A = at("jit1A", 22528, [128, 2048], BF16)
    jit2A = at("jit2A", 24576, [128, 2048], BF16)
    jit3A = at("jit3A", 26624, [128, 2048], BF16)
    cst_wqkv = at("cst_wqkv", 0, [128, 3072], F32)
    cst_rows = at("cst_rows", 6144, [1, 6400], F32)
    cst_wout = at("cst_wout", 19072, [128, 128], F32)

    ps_g = nc.alloc_psum_tensor("ps_g", [128, 1024], F32).ap()
    ps_u = nc.alloc_psum_tensor("ps_u", [128, 1024], F32).ap()
    ps_m = nc.alloc_psum_tensor("ps_m", [128, 1024], F32).ap()
    ps_tr = nc.alloc_psum_tensor("ps_tr", [128, 2048], BF16).ap()

    class Ctr:
        def __init__(self, name):
            self.h = nc.alloc_semaphore(name)
            self.n = 0

    s_pe, s_ve, s_se, s_po = Ctr("s_pe"), Ctr("s_ve"), Ctr("s_se"), Ctr("s_po")
    s_dc = Ctr("s_dc")
    s_rs = [Ctr(f"s_rs{i}") for i in range(2)]
    s_ws = [Ctr(f"s_ws{i}") for i in range(4)]

    prog = {k: [] for k in (SYNC, PE, VEC, SCA, POOL)}

    def em(eng, fn):
        prog[eng].append(fn)

    def wait(eng, ctr, val):
        if val > 0:
            em(eng, lambda e, h=ctr.h, v=val: e.wait_ge(h, v))

    def pe_inc(fn):
        s_pe.n += 1
        em(PE, lambda e, f=fn: f(e).then_inc(s_pe.h, 1))
        return s_pe.n

    def ve_inc(fn):
        s_ve.n += 1
        em(VEC, lambda e, f=fn: f(e).then_inc(s_ve.h, 1))
        return s_ve.n

    def se_inc(fn):
        s_se.n += 1
        em(SCA, lambda e, f=fn: f(e).then_inc(s_se.h, 1))
        return s_se.n

    def sync_dma(ctr, dst, src):
        ctr.n += 16
        em(SYNC, lambda e, d=dst, s=src: e.dma_start(out=d, in_=s).then_inc(ctr.h, 16))
        return ctr.n

    def pool_dma(ctr, dst, src):
        ctr.n += 16
        em(POOL, lambda e, d=dst, s=src: e.dma_start(out=d, in_=s).then_inc(ctr.h, 16))
        return ctr.n

    def vsync():
        wait(VEC, s_ve, s_ve.n)

    def ssync():
        wait(SCA, s_se, s_se.n)

    cv = sync_dma(s_dc, cst_wqkv[:], d_wqkv[:])
    cv = sync_dma(s_dc, cst_rows[:], d_rows[:])
    cv = sync_dma(s_dc, cst_wout[:], d_wout[:])
    cv = sync_dma(s_dc, ident[:], d_ident[:])
    cv = sync_dma(s_dc, scales[:, 0:144], d_scales[:])
    wait(VEC, s_dc, cv)
    ve_inc(lambda e: e.tensor_copy(wqkv_sb[:], cst_wqkv[:]))
    ve_inc(lambda e: e.tensor_copy(rows[:], cst_rows[:]))
    ve_inc(lambda e: e.tensor_copy(woutbd[:], cst_wout[:]))
    ve_inc(lambda e: e.tensor_scalar(
        scales[:, 144:208], scales[:, 0:64], 0.5, None, ALU.mult))
    ve_inc(lambda e: e.memset(cns[:, 0:1], -0.5))
    ve_inc(lambda e: e.memset(cns[:, 1:2], -1.0))
    ve_inc(lambda e: e.memset(cns[:, 2:3], -128.0))
    ve_inc(lambda e: e.memset(cns[:, 3:4], 1.0))
    ve_inc(lambda e: e.memset(cns[:, 4:5], 0.125))
    ve_inc(lambda e: e.memset(onesc[:], 1.0))
    consts_done = ve_inc(lambda e: e.memset(ones_fr[:], 1.0))
    for eng in (PE, SCA, POOL):
        wait(eng, s_ve, consts_done)

    bqkv_row = rows[:, 0:6144]
    bout_row = rows[:, 6144:6272]
    ones_row = rows[:, 6272:6400]

    gsc = lambda j: scales[:, j:j + 1]
    usc = lambda j: scales[:, 64 + j:64 + j + 1]
    dsc = lambda j: scales[:, 128 + j:128 + j + 1]
    hgsc = lambda j: scales[:, 144 + j:144 + j + 1]

    def emit_ln_stats(src_ap, st, dump, sca_wait=None):
        C = lambda i: stats[:, st * 8 + i:st * 8 + i + 1]
        if sca_wait is not None:
            wait(SCA, sca_wait[0], sca_wait[1])
        v1 = se_inc(lambda e, s=src_ap, d=dump: e.activation(d, s, AF.Square, accum_out=C(0)))
        ve_inc(lambda e, s=src_ap: e.reduce_sum(C(1), s, axis=AX.X))
        wait(VEC, s_se, v1)
        vsync()
        ve_inc(lambda e: e.tensor_scalar(C(2), C(1), -INV_D, None, ALU.mult))
        vsync()
        ve_inc(lambda e: e.tensor_tensor(C(3), C(2), C(2), ALU.mult))
        ve_inc(lambda e: e.tensor_scalar(C(4), C(0), INV_D, EPS, ALU.mult, ALU.add))
        vsync()
        v2 = ve_inc(lambda e: e.tensor_tensor(C(4), C(4), C(3), ALU.subtract))
        wait(SCA, s_ve, v2)
        se_inc(lambda e: e.activation(C(4), C(4), AF.Ln))
        ssync()
        return se_inc(lambda e: e.activation(C(5), C(4), AF.Exp, scale=cns[:, 0:1]))

    def negm_of(st):
        return stats[:, st * 8 + 2:st * 8 + 3]

    def rstd_of(st):
        return stats[:, st * 8 + 5:st * 8 + 6]

    tr_state = [0, 0]
    tr_next = [0]

    def emit_transposes(pairs, first_wait=None):
        last = 0
        for srcs, dst_emit in pairs:
            slot = tr_next[0]
            tr_next[0] ^= 1
            if first_wait is not None:
                wait(PE, first_wait[0], first_wait[1])
                first_wait = None
            wait(PE, s_se, tr_state[slot])
            for k, src in enumerate(srcs):
                dstp = ps_tr[:, slot * 1024 + k * 128: slot * 1024 + (k + 1) * 128]
                if k == 3:
                    pe_inc(lambda e, d=dstp, s=src: e.transpose(d, s, ident[:]))
                else:
                    em(PE, lambda e, d=dstp, s=src: e.transpose(d, s, ident[:]))
            wait(SCA, s_pe, s_pe.n)
            slot_ap = ps_tr[:, slot * 1024: slot * 1024 + 512]
            last = se_inc(lambda e, sa=slot_ap, de=dst_emit: de(e, sa))
            tr_state[slot] = last
        return last

    def transpose16(src_sb, dst_for_group, first_wait=None):
        pairs = []
        for grp in range(4):
            srcs = [src_sb[:, (grp * 4 + k) * 128:(grp * 4 + k + 1) * 128] for k in range(4)]
            pairs.append((srcs, dst_for_group(grp)))
        return emit_transposes(pairs, first_wait=first_wait)

    def mk_dst_plain(dst_base):
        def f(grp):
            def emitfn(e, slot_ap, g=grp, db=dst_base):
                return e.copy(db[:, g * 512:(g + 1) * 512], slot_ap)
            return emitfn
        return f

    psm_state = [(s_ve, 0), (s_ve, 0)]

    rs_dma = [0, 0]
    rnext = [0]

    pe_after_down = [0]
    wslot_guard = [0, 0, 0, 0]
    dn_guard = [0, 0, 0, 0]
    xf_ready = [0, 0]

    import os as _os
    KATT = int(_os.environ.get("KATT", "9"))

    def attn_tile(h, tt, LA):
        g_tt = h * NTT + tt
        k = rnext[0] % 2
        rnext[0] += 1
        slot = rsl[k]
        cg = LA["cg"]
        ncg = 32 // cg
        ebs = LA["ebs"]
        nb = len(ebs)

        wait(SYNC, s_rs[k], rs_dma[k])
        xv = sync_dma(s_rs[k], slot, d_x[g_tt * 128:(g_tt + 1) * 128, :])
        rs_dma[k] = xv
        yield 0.5

        wait(VEC, s_rs[k], xv)
        rs1 = emit_ln_stats(slot, 0, tm_scr[:], sca_wait=(s_rs[k], xv))
        wait(VEC, s_se, rs1)
        ln1_done = ve_inc(lambda e: e.tensor_scalar(
            tm_scr[:], slot, negm_of(0), rstd_of(0), ALU.add, ALU.mult))
        yield 4.0
        if KATT < 3:
            return

        ln1T = LA["ln1T"]
        ev_ln1T = transpose16(tm_scr, mk_dst_plain(ln1T), first_wait=(s_ve, ln1_done))
        yield 1.5
        if KATT < 4:
            return

        qkv = LA["qkv"]
        for p in range(16):
            bank = p % 2
            wait(PE, psm_state[bank][0], psm_state[bank][1])
            if p == 0:
                wait(PE, s_se, ev_ln1T)
            dst = ps_m[:, bank * 512: bank * 512 + 384]
            em(PE, lambda e, d=dst, pp=p: e.matmul(
                d[:, 0:192], ln1T[0:64, pp * 128:(pp + 1) * 128],
                wqkv_sb[0:64, pp * 192:(pp + 1) * 192], start=True, stop=False))
            em(PE, lambda e, d=dst, pp=p: e.matmul(
                d[:, 0:192], ones_row, bqkv_row[:, 2 * pp * 192:(2 * pp + 1) * 192],
                start=False, stop=True))
            em(PE, lambda e, d=dst, pp=p: e.matmul(
                d[:, 192:384], ln1T[64:128, pp * 128:(pp + 1) * 128],
                wqkv_sb[64:128, pp * 192:(pp + 1) * 192], start=True, stop=False))
            pv = pe_inc(lambda e, d=dst, pp=p: e.matmul(
                d[:, 192:384], ones_row, bqkv_row[:, (2 * pp + 1) * 192:(2 * pp + 2) * 192],
                start=False, stop=True))
            wait(VEC, s_pe, pv)
            evn = ve_inc(lambda e, d=dst, pp=p: e.tensor_copy(
                qkv[:, pp * 384:(pp + 1) * 384], d))
            psm_state[bank] = (s_ve, evn)
            if p % 4 == 3:
                yield 1.5
        if KATT < 5:
            return

        blob = qkv.rearrange("p (c three d) -> p c three d", c=32, three=3)
        e4s = [b.rearrange("p (c e d) -> p c e d", c=cg, e=32) for b in ebs]
        zs4 = LA["zscr"].rearrange("p (c e d) -> p c e d", c=cg, e=16)
        zf4 = zf[:, 0:cg * 64].rearrange("p (c o d) -> p c o d", c=cg, o=1)
        zfc = zf[:, 0:cg * 64].rearrange("p (c d) -> p c d", c=cg)
        ctx4 = ctx_f.rearrange("p (o e d) -> p o e d", o=1, e=32)
        kview = blob[:, :, 1, :].unsqueeze(1).broadcast_to([128, cg, 32, D])

        prod_c = [0] * ncg
        exp_c = [0] * ncg
        rz_c = [0] * ncg

        def emit_product(g):
            e4 = e4s[g % nb]
            qv = blob[:, g * cg:(g + 1) * cg, 0, :].unsqueeze(2).broadcast_to([128, cg, 32, D])
            prod_c[g] = ve_inc(lambda e, t=e4, q=qv: e.tensor_tensor(t, q, kview, ALU.mult))
            wait(SCA, s_ve, prod_c[g])
            fl = ebs[g % nb]
            exp_c[g] = se_inc(lambda e, f=fl: e.activation(f[:], f[:], AF.Exp, scale=cns[:, 4:5]))

        def emit_midA(g):
            e4 = e4s[g % nb]
            wait(VEC, s_se, exp_c[g])
            em(VEC, lambda e, t=e4: e.tensor_tensor(zs4, t[:, :, 0:16, :], t[:, :, 16:32, :], ALU.add))
            em(VEC, lambda e: e.tensor_tensor(zs4[:, :, 0:8, :], zs4[:, :, 0:8, :], zs4[:, :, 8:16, :], ALU.add))
            ve_inc(lambda e: e.tensor_tensor(zs4[:, :, 0:4, :], zs4[:, :, 0:4, :], zs4[:, :, 4:8, :], ALU.add))
            vsync()
            ve_inc(lambda e: e.tensor_tensor(zs4[:, :, 0:2, :], zs4[:, :, 0:2, :], zs4[:, :, 2:4, :], ALU.add))
            vsync()
            zv = ve_inc(lambda e: e.tensor_tensor(zf4, zs4[:, :, 0:1, :], zs4[:, :, 1:2, :], ALU.add))
            wait(SCA, s_ve, zv)
            se_inc(lambda e: e.activation(zf[:, 0:cg * 64], zf[:, 0:cg * 64], AF.Ln))
            ssync()
            rz_c[g] = se_inc(lambda e: e.activation(zf[:, 0:cg * 64], zf[:, 0:cg * 64], AF.Exp, scale=cns[:, 1:2]))

        def emit_midB(g):
            e4 = e4s[g % nb]
            ebflat = ebs[g % nb]
            wait(VEC, s_se, rz_c[g])
            vsl = blob[:, g * cg:(g + 1) * cg, 2, :]
            ve_inc(lambda e, v=vsl: e.tensor_tensor(v, v, zfc, ALU.mult))
            vsync()
            wb = vsl.unsqueeze(2).broadcast_to([128, cg, 32, D])
            em(VEC, lambda e, t=e4, w=wb: e.tensor_tensor(t, w, t, ALU.mult))
            if cg == 4:
                em(VEC, lambda e, t=e4: e.tensor_tensor(t[:, 0:2], t[:, 0:2], t[:, 2:4], ALU.add))
            if g == 0:
                ve_inc(lambda e, t=e4: e.tensor_tensor(ctx4, t[:, 0:1], t[:, 1:2], ALU.add))
            else:
                em(VEC, lambda e, t=e4: e.tensor_tensor(t[:, 0:1], t[:, 0:1], t[:, 1:2], ALU.add))
                ve_inc(lambda e, f=ebflat: e.tensor_tensor(ctx_f[:], ctx_f[:], f[:, 0:2048], ALU.add))

        if nb >= 3:
            emit_product(0)
            emit_product(1)
            yield 5.0
            emit_midA(0)
            yield 5.0
            for g in range(1, ncg):
                if g + 1 < ncg:
                    emit_product(g + 1)
                    yield 4.5
                emit_midB(g - 1)
                yield 6.0
                emit_midA(g)
                yield 5.0
            emit_midB(ncg - 1)
            yield 6.0
        else:
            emit_product(0)
            emit_product(1)
            yield 4.5
            for g in range(ncg):
                emit_midA(g)
                yield 3.0
                emit_midB(g)
                if g + 2 < ncg:
                    emit_product(g + 2)
                yield 5.5

        if KATT < 6:
            return
        ctx_bf = LA["ctx_bf"]
        cvv = ve_inc(lambda e: e.tensor_copy(ctx_bf[:], ctx_f[:]))
        ctxT = LA["ctxT"]
        ev_ctxT = transpose16(ctx_bf, mk_dst_plain(ctxT), first_wait=(s_ve, cvv))
        yield 3.0

        attnout = LA["attnout"]
        op_evacs = []
        for mg in range(4):
            bank = mg % 2
            wait(PE, psm_state[bank][0], psm_state[bank][1])
            if mg == 0:
                wait(PE, s_se, ev_ctxT)
            for kk in range(4):
                m2 = mg * 4 + kk
                dstp = ps_m[:, bank * 512 + kk * 128: bank * 512 + (kk + 1) * 128]
                em(PE, lambda e, d=dstp, mm=m2: e.matmul(
                    d, ctxT[:, mm * 128:(mm + 1) * 128], woutbd[:], start=True, stop=False))
                if kk == 3:
                    pe_inc(lambda e, d=dstp: e.matmul(
                        d, ones_row, bout_row, start=False, stop=True))
                else:
                    em(PE, lambda e, d=dstp: e.matmul(
                        d, ones_row, bout_row, start=False, stop=True))
            wait(SCA, s_pe, s_pe.n)
            evn = se_inc(lambda e, b=bank, m=mg: e.copy(
                attnout[:, m * 512:(m + 1) * 512], ps_m[:, b * 512:(b + 1) * 512]))
            psm_state[bank] = (s_se, evn)
            op_evacs.append(evn)
        yield 2.0

        if KATT < 7:
            return
        wait(VEC, s_se, op_evacs[-1])
        rs2 = emit_ln_stats(attnout[:], 0, tm_scr[:], sca_wait=(s_se, op_evacs[-1]))
        wait(VEC, s_se, rs2)
        em(VEC, lambda e: e.tensor_scalar(ctx_f[:], attnout[:], negm_of(0), rstd_of(0),
                                          ALU.add, ALU.mult))
        r1v = ve_inc(lambda e: e.tensor_tensor(slot, slot, ctx_f[:], ALU.add))
        yield 5.0

        if KATT < 8:
            return
        rs3 = emit_ln_stats(slot, 0, tm_scr[:], sca_wait=(s_ve, r1v))
        wait(VEC, s_se, rs3)
        ln3_done = ve_inc(lambda e: e.tensor_scalar(
            tm_scr[:], slot, negm_of(0), rstd_of(0), ALU.add, ALU.mult))
        yield 4.0

        xfv = xf[h].rearrange("p (f t) -> p f t", f=16)

        def xf_dst(grp):
            def emitfn(e, slot_ap, g=grp, t0=tt):
                return e.copy(xfv[:, g * 4:(g + 1) * 4, t0 * 128:(t0 + 1) * 128],
                              slot_ap.rearrange("p (b f) -> p b f", b=4))
            return emitfn

        xf_ready[h] = transpose16(tm_scr, xf_dst, first_wait=(s_ve, ln3_done))

        if KATT < 9:
            return
        wait(SYNC, s_ve, ln3_done)
        rs_dma[k] = sync_dma(s_rs[k], d_r1[g_tt * 128:(g_tt + 1) * 128, :], slot)
        yield 2.0

    def ffn_half(h, pump):
        xffnT = xf[h]
        srow = ps_u[0:1, 0:512]

        wait(PE, s_se, xf_ready[h])
        wait(PE, s_se, dn_guard[2])
        sx_pe = 0
        for f in range(16):
            rhs = xffnT[:, f * 512:(f + 1) * 512]
            if f == 15:
                sx_pe = pe_inc(lambda e, r=rhs: e.matmul(srow, onesc[:], r, start=False, stop=True))
            else:
                em(PE, lambda e, r=rhs, st=(f == 0): e.matmul(srow, onesc[:], r, start=st, stop=False))
        wait(VEC, s_pe, sx_pe)
        corrx_done = ve_inc(lambda e: e.tensor_scalar(crow[:], srow, -128.0, None, ALU.mult))
        pump(2.0)

        pe_g = [0] * 64
        pe_u = [0] * 64
        hdone = [0] * 64
        for j in range(64):
            gslot = wstage[:, (j % 2) * 2048:(j % 2) * 2048 + 2048]
            uslot = wstage[:, (2 + j % 2) * 2048:(2 + j % 2) * 2048 + 2048]
            if j < 2:
                wait(POOL, s_pe, wslot_guard[j % 2])
            else:
                wait(POOL, s_pe, pe_g[j - 2])
            gv = pool_dma(s_ws[j % 2], gslot, d_wg[j])
            if j < 2:
                wait(POOL, s_pe, wslot_guard[2 + j % 2])
            else:
                wait(POOL, s_pe, pe_u[j - 2])
            uv = pool_dma(s_ws[2 + j % 2], uslot, d_wu[j])

            wait(PE, s_ws[j % 2], gv)
            if j >= 2:
                wait(PE, s_ve, hdone[j - 2])
            else:
                wait(PE, s_se, dn_guard[j])
            gb = ps_g[:, (j % 2) * 512:(j % 2) * 512 + 512]
            ub = ps_u[:, (j % 2) * 512:(j % 2) * 512 + 512]
            for f in range(16):
                lhs = gslot[:, f * 128:(f + 1) * 128]
                rhs = xffnT[:, f * 512:(f + 1) * 512]
                em(PE, lambda e, d=gb, l=lhs, r=rhs, st=(f == 0): e.matmul(d, l, r, start=st, stop=False))
            if j == 0:
                wait(PE, s_ve, corrx_done)
            pe_g[j] = pe_inc(lambda e, d=gb: e.matmul(d, ones_row, crow[:], start=False, stop=True))
            wait(PE, s_ws[2 + j % 2], uv)
            if j < 2:
                wait(PE, s_se, dn_guard[2 + j])
            for f in range(16):
                lhs = uslot[:, f * 128:(f + 1) * 128]
                rhs = xffnT[:, f * 512:(f + 1) * 512]
                em(PE, lambda e, d=ub, l=lhs, r=rhs, st=(f == 0): e.matmul(d, l, r, start=st, stop=False))
            pe_u[j] = pe_inc(lambda e, d=ub: e.matmul(d, ones_row, crow[:], start=False, stop=True))

            sp = (j % 2) * 512
            if j >= 2:
                wait(SCA, s_ve, hdone[j - 2])
            wait(SCA, s_pe, pe_g[j])
            se_inc(lambda e, g=gb, jj=j, s=sp: e.activation(
                scr1[:, s:s + 512], g, AF.Identity, scale=hgsc(jj)))
            se_inc(lambda e, s=sp: e.activation(
                scr2[:, s:s + 512], scr1[:, s:s + 512], AF.Tanh))
            th1_v = se_inc(lambda e, s=sp: e.activation(
                scr2[:, s:s + 512], scr2[:, s:s + 512], AF.Identity, bias=cns[:, 3:4]))
            wait(VEC, s_se, th1_v)
            ve_inc(lambda e, s=sp: e.tensor_tensor(
                scr1[:, s:s + 512], scr1[:, s:s + 512], scr2[:, s:s + 512], ALU.mult))
            vsync()
            wait(VEC, s_pe, pe_u[j])
            hdone[j] = ve_inc(lambda e, u=ub, jj=j, s=sp: e.scalar_tensor_tensor(
                h_sb[:, jj * 512:(jj + 1) * 512], u, usc(jj), scr1[:, s:s + 512], ALU.mult, ALU.mult))
            pump(4.5)

        wait(PE, s_ve, hdone[63])
        sh_pe = 0
        for i in range(64):
            r = h_sb[:, i * 512:(i + 1) * 512]
            if i == 63:
                sh_pe = pe_inc(lambda e, rr=r: e.matmul(srow, onesc[:], rr, start=False, stop=True))
            else:
                em(PE, lambda e, rr=r, st=(i == 0): e.matmul(srow, onesc[:], rr, start=st, stop=False))
        wait(VEC, s_pe, sh_pe)
        corrh_done = ve_inc(lambda e: e.tensor_scalar(crow_f[:], srow, -128.0, None, ALU.mult))

        NQ = 64
        dn_dv = [0] * NQ
        pe_q = [0] * NQ
        dvac = [0] * 16
        pe_dn = [0] * 16
        dTv = xffnT.rearrange("p (t f) -> p t f", t=NTT)

        def emit_down_dma(q):
            hhq, qq = q // 4, q % 4
            slot = wstage[:, (q % 4) * 2048:(q % 4) * 2048 + 2048]
            if q < 4:
                first_guards = [pe_g[62], pe_g[63], pe_u[62], pe_u[63]]
                wait(POOL, s_pe, first_guards[q])
            else:
                wait(POOL, s_pe, pe_q[q - 4])
            dn_dv[q] = pool_dma(s_ws[q % 4], slot, d_wd[hhq][:, qq * 2048:(qq + 1) * 2048])

        def emit_dn_transpose(hh):
            dslot = dsb[:, (hh % 2) * 512:(hh % 2) * 512 + 512]

            def emitfn(e, slot_ap, j2=hh):
                return e.copy(dTv[:, :, j2 * 128:(j2 + 1) * 128],
                              slot_ap.rearrange("p (b f) -> p b f", b=4))

            srcs = [dslot[:, b * 128:(b + 1) * 128] for b in range(4)]
            emit_transposes([(srcs, emitfn)], first_wait=(s_se, dvac[hh]))

        emit_down_dma(0)
        emit_down_dma(1)
        dn_banks = [ps_g[:, 0:512], ps_g[:, 512:1024], ps_u[:, 0:512], ps_u[:, 512:1024]]
        for hh in range(16):
            db = dn_banks[hh % 4]
            if hh >= 4:
                wait(PE, s_se, dvac[hh - 4])
            if hh == 0:
                wait(PE, s_ve, corrh_done)
            for qq in range(4):
                q = hh * 4 + qq
                if q + 2 < NQ:
                    emit_down_dma(q + 2)
                wait(PE, s_ws[q % 4], dn_dv[q])
                for i2 in range(16):
                    i = qq * 16 + i2
                    lhs = wstage[:, (q % 4) * 2048 + i2 * 128: (q % 4) * 2048 + (i2 + 1) * 128]
                    rhs = h_sb[:, i * 512:(i + 1) * 512]
                    if i2 == 15:
                        pe_q[q] = pe_inc(lambda e, d=db, l=lhs, r=rhs: e.matmul(
                            d, l, r, start=False, stop=False))
                    else:
                        em(PE, lambda e, d=db, l=lhs, r=rhs, st=(i == 0): e.matmul(d, l, r, start=st, stop=False))
            pe_dn[hh] = pe_inc(lambda e, d=db: e.matmul(
                d, ones_fr[:], crow_f[:], start=False, stop=True))
            wait(SCA, s_pe, pe_dn[hh])
            dvac[hh] = se_inc(lambda e, d=dsb[:, (hh % 2) * 512:(hh % 2) * 512 + 512],
                              s=db, jj=hh: e.activation(d, s, AF.Identity, scale=dsc(jj)))
            if hh >= 1:
                emit_dn_transpose(hh - 1)
            pump(28.0)
        emit_dn_transpose(15)
        pe_after_down[0] = s_pe.n
        for b in range(4):
            dn_guard[b] = dvac[12 + b]
            wslot_guard[b] = pe_q[60 + b]

    def emit_ln4(h):
        last_dn = max(tr_state)
        dump1 = tm_scr[:]
        wait(VEC, s_se, last_dn)
        for tt in range(NTT):
            g_tt = h * NTT + tt
            k = rnext[0] % 2
            rnext[0] += 1
            slot = rsl[k]
            wait(SYNC, s_rs[k], rs_dma[k])
            rbv = sync_dma(s_rs[k], slot, d_r1[g_tt * 128:(g_tt + 1) * 128, :])
            rs_dma[k] = rbv
            dT_tt = xf[h][:, tt * 2048:(tt + 1) * 2048]
            rs4 = emit_ln_stats(dT_tt, 1, dump1, sca_wait=(s_se, last_dn) if tt == 0 else None)
            yield 12.0
            wait(VEC, s_se, rs4)
            em(VEC, lambda e, d=dT_tt: e.tensor_scalar(ctx_f[:], d, negm_of(1), rstd_of(1),
                                                       ALU.add, ALU.mult))
            wait(VEC, s_rs[k], rbv)
            r2v = ve_inc(lambda e, r=slot: e.tensor_tensor(r, r, ctx_f[:], ALU.add))
            wait(SYNC, s_ve, r2v)
            rs_dma[k] = sync_dma(s_rs[k], d_y[g_tt * 128:(g_tt + 1) * 128, :], slot)
            yield 12.0

    LAYA = dict(cg=4, ebs=[ebA0, ebA1, ebA2], qkv=qkvA, zscr=zscrA,
                ln1T=jit1A, ctx_bf=jit1A, ctxT=jit2A, attnout=jit3A)
    LAYB = dict(cg=2, ebs=[ebB0, ebB1], qkv=qkvB, zscr=zscrB,
                ln1T=jit1B, ctx_bf=jit1B, ctxT=ctxTB, attnout=attnoutB)

    def make_pump(it):
        state = {"alive": True, "acc": 0.0}

        def pump(budget):
            state["acc"] += budget
            while state["alive"] and state["acc"] > 0.0:
                try:
                    state["acc"] -= next(it)
                except StopIteration:
                    state["alive"] = False

        def drain():
            while state["alive"]:
                try:
                    next(it)
                except StopIteration:
                    state["alive"] = False
        return pump, drain

    for tt in range(NTT):
        for _ in attn_tile(0, tt, LAYA):
            pass

    def gen_b():
        for tt in range(NTT):
            yield from attn_tile(1, tt, LAYB)

    pump_b, drain_b = make_pump(gen_b())
    ffn_half(0, pump_b)
    drain_b()
    pump_4, drain_4 = make_pump(emit_ln4(0))
    ffn_half(1, pump_4)
    drain_4()
    for _ in emit_ln4(1):
        pass

    for k in range(2):
        wait(SYNC, s_rs[k], rs_dma[k])

    with nc.Block() as block:
        @block.sync
        def _(eng):
            for fn in prog[SYNC]:
                fn(eng)

        @block.tensor
        def _(eng):
            for fn in prog[PE]:
                fn(eng)

        @block.vector
        def _(eng):
            for fn in prog[VEC]:
                fn(eng)

        @block.scalar
        def _(eng):
            for fn in prog[SCA]:
                fn(eng)

        @block.gpsimd
        def _(eng):
            for fn in prog[POOL]:
                fn(eng)

    return nc



def _prep_weights(Wqkv, bqkv, Wout, bout, gate_w, gate_scale, up_w, up_scale,
                  down_w, down_scale):
    f32 = np.float32
    wqkv = np.asarray(Wqkv, dtype=f32)
    wqkv_p = np.ascontiguousarray(
        wqkv.reshape(16, 2, 64, 192).transpose(1, 2, 0, 3)).reshape(128, 3072)

    rows = np.zeros((1, 6400), dtype=f32)
    rows[0, 0:6144] = np.asarray(bqkv, dtype=f32).reshape(32, 192).reshape(-1)
    bo = np.asarray(bout, dtype=f32).reshape(-1)
    rows[0, 6144:6208] = bo
    rows[0, 6208:6272] = bo
    rows[0, 6272:6400] = 1.0

    wo = np.asarray(Wout, dtype=f32)
    wout_bd = np.zeros((128, 128), dtype=f32)
    wout_bd[0:64, 0:64] = wo
    wout_bd[64:128, 64:128] = wo

    ident = np.eye(128, dtype=f32).astype(ml_dtypes.bfloat16)

    scales = np.zeros((128, 144), dtype=f32)
    scales[:, 0:64] = np.asarray(gate_scale, dtype=f32).reshape(64, 128).T
    scales[:, 64:128] = np.asarray(up_scale, dtype=f32).reshape(64, 128).T
    scales[:, 128:144] = np.asarray(down_scale, dtype=f32).reshape(16, 128).T

    gw = np.asarray(gate_w, dtype=np.int32)
    uw = np.asarray(up_w, dtype=np.int32)
    dw = np.asarray(down_w, dtype=np.int32)
    wg_s = np.ascontiguousarray(gw.reshape(64, 128, 16, 128).transpose(0, 3, 2, 1)).reshape(64, 128, 2048)
    wu_s = np.ascontiguousarray(uw.reshape(64, 128, 16, 128).transpose(0, 3, 2, 1)).reshape(64, 128, 2048)
    wd_s = np.ascontiguousarray(dw.reshape(16, 128, 64, 128).transpose(0, 3, 2, 1)).reshape(16, 128, 8192)

    return dict(wqkv_p=wqkv_p, rows_f32=rows, wout_bd=wout_bd, identity=ident,
                scales=scales, wg_s=wg_s, wu_s=wu_s, wd_s=wd_s)


def kernel(L0_attn_x, L0_attn_Wqkv, L0_attn_bqkv, L0_attn_Wout, L0_attn_bout,
           L0_ffn_gate_w, L0_ffn_gate_scale, L0_ffn_up_w, L0_ffn_up_scale,
           L0_ffn_down_w, L0_ffn_down_scale):
    x = np.ascontiguousarray(np.asarray(L0_attn_x, dtype=np.float32).reshape(8192, 2048))
    shared = _prep_weights(L0_attn_Wqkv, L0_attn_bqkv, L0_attn_Wout, L0_attn_bout,
                           L0_ffn_gate_w, L0_ffn_gate_scale, L0_ffn_up_w, L0_ffn_up_scale,
                           L0_ffn_down_w, L0_ffn_down_scale)
    nc = build_program()
    in_maps = []
    for c in range(NCORES):
        m = dict(shared)
        m["x"] = np.ascontiguousarray(x[c * TOK:(c + 1) * TOK])
        in_maps.append(m)
    res = run_bass_kernel_spmd(nc, in_maps, core_ids=list(range(NCORES)))
    import os as _os1
    if _os1.environ.get("KR1"):
        globals()["_dbg_r1"] = [r.get("r1_scratch") for r in res.results]
    y = np.concatenate([r["y"] for r in res.results], axis=0)
    return np.ascontiguousarray(y.reshape(1, 8192, 2048).astype(np.float32))


# version 23
# speedup vs baseline: 1.0065x; 1.0065x over previous
import numpy as np
import ml_dtypes

import concourse.bass as bass
import concourse.mybir as mybir
from concourse.bass_utils import run_bass_kernel_spmd

F32 = mybir.dt.float32
BF16 = mybir.dt.bfloat16
I32 = mybir.dt.int32
AF = mybir.ActivationFunctionType
ALU = mybir.AluOpType
AX = mybir.AxisListType

NCORES = 8
TOK = 1024
NH = 2
NTT = 4
D = 64
EPS = 1e-5
INV_D = 1.0 / 2048.0

SYNC, PE, VEC, SCA, POOL = "sync", "tensor", "vector", "scalar", "gpsimd"


def build_program():
    nc = bass.Bass("TRN2", target_bir_lowering=False, debug=False, detect_race_conditions=False)

    d_x = nc.dram_tensor("x", [TOK, 2048], F32, kind="ExternalInput").ap()
    d_y = nc.dram_tensor("y", [TOK, 2048], F32, kind="ExternalOutput").ap()
    d_wqkv = nc.dram_tensor("wqkv_p", [128, 3072], F32, kind="ExternalInput").ap()
    d_rows = nc.dram_tensor("rows_f32", [1, 6400], F32, kind="ExternalInput").ap()
    d_wout = nc.dram_tensor("wout_bd", [128, 128], F32, kind="ExternalInput").ap()
    d_ident = nc.dram_tensor("identity", [128, 128], BF16, kind="ExternalInput").ap()
    d_scales = nc.dram_tensor("scales", [128, 144], F32, kind="ExternalInput").ap()
    d_wg = nc.dram_tensor("wg_s", [64, 128, 2048], I32, kind="ExternalInput").ap()
    d_wu = nc.dram_tensor("wu_s", [64, 128, 2048], I32, kind="ExternalInput").ap()
    d_wd = nc.dram_tensor("wd_s", [16, 128, 8192], I32, kind="ExternalInput").ap()
    import os as _os0
    _r1kind = "ExternalOutput" if _os0.environ.get("KR1") else "Internal"
    d_r1 = nc.dram_tensor("r1_scratch", [TOK, 2048], F32, kind=_r1kind).ap()

    A = lambda name, shape, dt: nc.alloc_sbuf_tensor(name, shape, dt)

    wqkv_sb = A("wqkv_sb", [128, 3072], BF16).ap()
    woutbd = A("woutbd", [128, 128], BF16).ap()
    ident = A("ident_sb", [128, 128], BF16).ap()
    scales = A("scales_sb", [128, 208], F32).ap()
    rows = A("rows_sb", [1, 6400], BF16).ap()
    stats = A("stats_sb", [128, 16], F32).ap()
    cns = A("cns_sb", [128, 6], F32).ap()
    onesc = A("onesc_sb", [128, 1], BF16).ap()
    ones_fr = A("ones_fr_sb", [1, 128], F32).ap()
    crow = A("crow_sb", [1, 512], BF16).ap()
    crow_f = A("crow_f_sb", [1, 512], F32).ap()
    zf = A("zf_sb", [128, 256], F32).ap()

    rsl = [A(f"rslot{k}", [128, 2048], F32).ap() for k in range(2)]
    tm_scr = A("tm_scr", [128, 2048], BF16).ap()
    xf = [A(f"xf{p}", [128, 16 * 512], BF16).ap() for p in range(2)]
    scr1 = A("scr1", [128, 1024], BF16).ap()
    scr2 = A("scr2", [128, 1024], BF16).ap()
    dsb = A("dsb", [128, 2 * 512], BF16).ap()
    wstage = A("wstage", [128, 8192], BF16).ap()

    qkvB_base = nc.sbuf_base
    qkvB = A("qkvB", [128, 6144], BF16).ap()
    ebB_base = nc.sbuf_base
    ebA2 = A("ebA2", [128, 8192], BF16).ap()
    ebB0 = nc.alloc_sbuf_tensor_at("ebB0", [128, 4096], BF16, offset=ebB_base).ap()
    ebB1 = nc.alloc_sbuf_tensor_at("ebB1", [128, 4096], BF16, offset=ebB_base + 8192).ap()
    zscrB = A("zscrB", [128, 2048], BF16).ap()
    ctx_f = A("ctx_f", [128, 2048], F32).ap()
    jit1B = A("jit1B", [128, 2048], BF16).ap()
    attnoutB = nc.alloc_sbuf_tensor_at("attnoutB", [128, 2048], BF16, offset=qkvB_base).ap()
    ctxTB = nc.alloc_sbuf_tensor_at("ctxTB", [128, 2048], BF16, offset=qkvB_base + 4096).ap()

    h_base = nc.sbuf_base
    h_sb = A("h_sb", [128, 64 * 512], BF16).ap()

    def at(name, off_bf16, shape, dt):
        return nc.alloc_sbuf_tensor_at(name, shape, dt, offset=h_base + 2 * off_bf16).ap()

    qkvA = at("qkvA", 0, [128, 6144], BF16)
    ebA0 = at("ebA0", 6144, [128, 8192], BF16)
    ebA1 = at("ebA1", 6144 + 8192, [128, 8192], BF16)
    zscrA = at("zscrA", 22528, [128, 4096], BF16)
    jit1A = at("jit1A", 22528, [128, 2048], BF16)
    jit2A = at("jit2A", 24576, [128, 2048], BF16)
    jit3A = at("jit3A", 26624, [128, 2048], BF16)
    cst_wqkv = at("cst_wqkv", 0, [128, 3072], F32)
    cst_rows = at("cst_rows", 6144, [1, 6400], F32)
    cst_wout = at("cst_wout", 19072, [128, 128], F32)

    ps_g = nc.alloc_psum_tensor("ps_g", [128, 1024], F32).ap()
    ps_u = nc.alloc_psum_tensor("ps_u", [128, 1024], F32).ap()
    ps_m = nc.alloc_psum_tensor("ps_m", [128, 1024], F32).ap()
    ps_tr = nc.alloc_psum_tensor("ps_tr", [128, 2048], BF16).ap()

    class Ctr:
        def __init__(self, name):
            self.h = nc.alloc_semaphore(name)
            self.n = 0

    s_pe, s_ve, s_se, s_po = Ctr("s_pe"), Ctr("s_ve"), Ctr("s_se"), Ctr("s_po")
    s_dc = Ctr("s_dc")
    s_rs = [Ctr(f"s_rs{i}") for i in range(2)]
    s_ws = [Ctr(f"s_ws{i}") for i in range(4)]

    prog = {k: [] for k in (SYNC, PE, VEC, SCA, POOL)}

    def em(eng, fn):
        prog[eng].append(fn)

    def wait(eng, ctr, val):
        if val > 0:
            em(eng, lambda e, h=ctr.h, v=val: e.wait_ge(h, v))

    def pe_inc(fn):
        s_pe.n += 1
        em(PE, lambda e, f=fn: f(e).then_inc(s_pe.h, 1))
        return s_pe.n

    def ve_inc(fn):
        s_ve.n += 1
        em(VEC, lambda e, f=fn: f(e).then_inc(s_ve.h, 1))
        return s_ve.n

    def se_inc(fn):
        s_se.n += 1
        em(SCA, lambda e, f=fn: f(e).then_inc(s_se.h, 1))
        return s_se.n

    def sync_dma(ctr, dst, src):
        ctr.n += 16
        em(SYNC, lambda e, d=dst, s=src: e.dma_start(out=d, in_=s).then_inc(ctr.h, 16))
        return ctr.n

    def pool_dma(ctr, dst, src):
        ctr.n += 16
        em(POOL, lambda e, d=dst, s=src: e.dma_start(out=d, in_=s).then_inc(ctr.h, 16))
        return ctr.n

    def vsync():
        wait(VEC, s_ve, s_ve.n)

    def ssync():
        wait(SCA, s_se, s_se.n)

    cv = sync_dma(s_dc, cst_wqkv[:], d_wqkv[:])
    cv = sync_dma(s_dc, cst_rows[:], d_rows[:])
    cv = sync_dma(s_dc, cst_wout[:], d_wout[:])
    cv = sync_dma(s_dc, ident[:], d_ident[:])
    cv = sync_dma(s_dc, scales[:, 0:144], d_scales[:])
    wait(VEC, s_dc, cv)
    ve_inc(lambda e: e.tensor_copy(wqkv_sb[:], cst_wqkv[:]))
    ve_inc(lambda e: e.tensor_copy(rows[:], cst_rows[:]))
    ve_inc(lambda e: e.tensor_copy(woutbd[:], cst_wout[:]))
    ve_inc(lambda e: e.tensor_scalar(
        scales[:, 144:208], scales[:, 0:64], 0.5, None, ALU.mult))
    ve_inc(lambda e: e.memset(cns[:, 0:1], -0.5))
    ve_inc(lambda e: e.memset(cns[:, 1:2], -1.0))
    ve_inc(lambda e: e.memset(cns[:, 2:3], -128.0))
    ve_inc(lambda e: e.memset(cns[:, 3:4], 1.0))
    ve_inc(lambda e: e.memset(cns[:, 4:5], 0.125))
    ve_inc(lambda e: e.memset(onesc[:], 1.0))
    consts_done = ve_inc(lambda e: e.memset(ones_fr[:], 1.0))
    for eng in (PE, SCA, POOL):
        wait(eng, s_ve, consts_done)

    bqkv_row = rows[:, 0:6144]
    bout_row = rows[:, 6144:6272]
    ones_row = rows[:, 6272:6400]

    gsc = lambda j: scales[:, j:j + 1]
    usc = lambda j: scales[:, 64 + j:64 + j + 1]
    dsc = lambda j: scales[:, 128 + j:128 + j + 1]
    hgsc = lambda j: scales[:, 144 + j:144 + j + 1]

    def emit_ln_stats(src_ap, st, dump, sca_wait=None):
        C = lambda i: stats[:, st * 8 + i:st * 8 + i + 1]
        if sca_wait is not None:
            wait(SCA, sca_wait[0], sca_wait[1])
        v1 = se_inc(lambda e, s=src_ap, d=dump: e.activation(d, s, AF.Square, accum_out=C(0)))
        ve_inc(lambda e, s=src_ap: e.reduce_sum(C(1), s, axis=AX.X))
        wait(VEC, s_se, v1)
        vsync()
        ve_inc(lambda e: e.tensor_scalar(C(2), C(1), -INV_D, None, ALU.mult))
        vsync()
        ve_inc(lambda e: e.tensor_tensor(C(3), C(2), C(2), ALU.mult))
        ve_inc(lambda e: e.tensor_scalar(C(4), C(0), INV_D, EPS, ALU.mult, ALU.add))
        vsync()
        v2 = ve_inc(lambda e: e.tensor_tensor(C(4), C(4), C(3), ALU.subtract))
        wait(SCA, s_ve, v2)
        se_inc(lambda e: e.activation(C(4), C(4), AF.Ln))
        ssync()
        return se_inc(lambda e: e.activation(C(5), C(4), AF.Exp, scale=cns[:, 0:1]))

    def negm_of(st):
        return stats[:, st * 8 + 2:st * 8 + 3]

    def rstd_of(st):
        return stats[:, st * 8 + 5:st * 8 + 6]

    tr_state = [0, 0]
    tr_next = [0]

    def emit_transposes(pairs, first_wait=None):
        last = 0
        for srcs, dst_emit in pairs:
            slot = tr_next[0]
            tr_next[0] ^= 1
            if first_wait is not None:
                wait(PE, first_wait[0], first_wait[1])
                first_wait = None
            wait(PE, s_se, tr_state[slot])
            for k, src in enumerate(srcs):
                dstp = ps_tr[:, slot * 1024 + k * 128: slot * 1024 + (k + 1) * 128]
                if k == 3:
                    pe_inc(lambda e, d=dstp, s=src: e.transpose(d, s, ident[:]))
                else:
                    em(PE, lambda e, d=dstp, s=src: e.transpose(d, s, ident[:]))
            wait(SCA, s_pe, s_pe.n)
            slot_ap = ps_tr[:, slot * 1024: slot * 1024 + 512]
            last = se_inc(lambda e, sa=slot_ap, de=dst_emit: de(e, sa))
            tr_state[slot] = last
        return last

    def transpose16(src_sb, dst_for_group, first_wait=None):
        pairs = []
        for grp in range(4):
            srcs = [src_sb[:, (grp * 4 + k) * 128:(grp * 4 + k + 1) * 128] for k in range(4)]
            pairs.append((srcs, dst_for_group(grp)))
        return emit_transposes(pairs, first_wait=first_wait)

    def mk_dst_plain(dst_base):
        def f(grp):
            def emitfn(e, slot_ap, g=grp, db=dst_base):
                return e.copy(db[:, g * 512:(g + 1) * 512], slot_ap)
            return emitfn
        return f

    psm_state = [(s_ve, 0), (s_ve, 0)]

    rs_dma = [0, 0]
    rnext = [0]

    pe_after_down = [0]
    wslot_guard = [0, 0, 0, 0]
    dn_guard = [0, 0, 0, 0]
    xf_ready = [0, 0]

    import os as _os
    KATT = int(_os.environ.get("KATT", "9"))

    def attn_tile(h, tt, LA):
        g_tt = h * NTT + tt
        k = rnext[0] % 2
        rnext[0] += 1
        slot = rsl[k]
        cg = LA["cg"]
        ncg = 32 // cg
        ebs = LA["ebs"]
        nb = len(ebs)

        wait(SYNC, s_rs[k], rs_dma[k])
        xv = sync_dma(s_rs[k], slot, d_x[g_tt * 128:(g_tt + 1) * 128, :])
        rs_dma[k] = xv
        yield 0.5

        wait(VEC, s_rs[k], xv)
        rs1 = emit_ln_stats(slot, 0, tm_scr[:], sca_wait=(s_rs[k], xv))
        wait(VEC, s_se, rs1)
        ln1_done = ve_inc(lambda e: e.tensor_scalar(
            tm_scr[:], slot, negm_of(0), rstd_of(0), ALU.add, ALU.mult))
        yield 4.0
        if KATT < 3:
            return

        ln1T = LA["ln1T"]
        ev_ln1T = transpose16(tm_scr, mk_dst_plain(ln1T), first_wait=(s_ve, ln1_done))
        yield 1.5
        if KATT < 4:
            return

        qkv = LA["qkv"]
        for p in range(16):
            bank = p % 2
            wait(PE, psm_state[bank][0], psm_state[bank][1])
            if p == 0:
                wait(PE, s_se, ev_ln1T)
            dst = ps_m[:, bank * 512: bank * 512 + 384]
            em(PE, lambda e, d=dst, pp=p: e.matmul(
                d[:, 0:192], ln1T[0:64, pp * 128:(pp + 1) * 128],
                wqkv_sb[0:64, pp * 192:(pp + 1) * 192], start=True, stop=False))
            em(PE, lambda e, d=dst, pp=p: e.matmul(
                d[:, 0:192], ones_row, bqkv_row[:, 2 * pp * 192:(2 * pp + 1) * 192],
                start=False, stop=True))
            em(PE, lambda e, d=dst, pp=p: e.matmul(
                d[:, 192:384], ln1T[64:128, pp * 128:(pp + 1) * 128],
                wqkv_sb[64:128, pp * 192:(pp + 1) * 192], start=True, stop=False))
            pv = pe_inc(lambda e, d=dst, pp=p: e.matmul(
                d[:, 192:384], ones_row, bqkv_row[:, (2 * pp + 1) * 192:(2 * pp + 2) * 192],
                start=False, stop=True))
            wait(VEC, s_pe, pv)
            evn = ve_inc(lambda e, d=dst, pp=p: e.tensor_copy(
                qkv[:, pp * 384:(pp + 1) * 384], d))
            psm_state[bank] = (s_ve, evn)
            if p % 4 == 3:
                yield 1.5
        if KATT < 5:
            return

        blob = qkv.rearrange("p (c three d) -> p c three d", c=32, three=3)
        e4s = [b.rearrange("p (c e d) -> p c e d", c=cg, e=32) for b in ebs]
        zs4 = LA["zscr"].rearrange("p (c e d) -> p c e d", c=cg, e=16)
        zf4 = zf[:, 0:cg * 64].rearrange("p (c o d) -> p c o d", c=cg, o=1)
        zfc = zf[:, 0:cg * 64].rearrange("p (c d) -> p c d", c=cg)
        ctx4 = ctx_f.rearrange("p (o e d) -> p o e d", o=1, e=32)
        kview = blob[:, :, 1, :].unsqueeze(1).broadcast_to([128, cg, 32, D])

        prod_c = [0] * ncg
        exp_c = [0] * ncg
        rz_c = [0] * ncg

        def emit_product(g):
            e4 = e4s[g % nb]
            qv = blob[:, g * cg:(g + 1) * cg, 0, :].unsqueeze(2).broadcast_to([128, cg, 32, D])
            prod_c[g] = ve_inc(lambda e, t=e4, q=qv: e.tensor_tensor(t, q, kview, ALU.mult))
            wait(SCA, s_ve, prod_c[g])
            fl = ebs[g % nb]
            exp_c[g] = se_inc(lambda e, f=fl: e.activation(f[:], f[:], AF.Exp, scale=cns[:, 4:5]))

        def emit_midA(g):
            e4 = e4s[g % nb]
            wait(VEC, s_se, exp_c[g])
            em(VEC, lambda e, t=e4: e.tensor_tensor(zs4, t[:, :, 0:16, :], t[:, :, 16:32, :], ALU.add))
            em(VEC, lambda e: e.tensor_tensor(zs4[:, :, 0:8, :], zs4[:, :, 0:8, :], zs4[:, :, 8:16, :], ALU.add))
            ve_inc(lambda e: e.tensor_tensor(zs4[:, :, 0:4, :], zs4[:, :, 0:4, :], zs4[:, :, 4:8, :], ALU.add))
            vsync()
            ve_inc(lambda e: e.tensor_tensor(zs4[:, :, 0:2, :], zs4[:, :, 0:2, :], zs4[:, :, 2:4, :], ALU.add))
            vsync()
            zv = ve_inc(lambda e: e.tensor_tensor(zf4, zs4[:, :, 0:1, :], zs4[:, :, 1:2, :], ALU.add))
            wait(SCA, s_ve, zv)
            se_inc(lambda e: e.activation(zf[:, 0:cg * 64], zf[:, 0:cg * 64], AF.Ln))
            ssync()
            rz_c[g] = se_inc(lambda e: e.activation(zf[:, 0:cg * 64], zf[:, 0:cg * 64], AF.Exp, scale=cns[:, 1:2]))

        def emit_midB(g):
            e4 = e4s[g % nb]
            ebflat = ebs[g % nb]
            wait(VEC, s_se, rz_c[g])
            vsl = blob[:, g * cg:(g + 1) * cg, 2, :]
            ve_inc(lambda e, v=vsl: e.tensor_tensor(v, v, zfc, ALU.mult))
            vsync()
            wb = vsl.unsqueeze(2).broadcast_to([128, cg, 32, D])
            em(VEC, lambda e, t=e4, w=wb: e.tensor_tensor(t, w, t, ALU.mult))
            if cg == 4:
                em(VEC, lambda e, t=e4: e.tensor_tensor(t[:, 0:2], t[:, 0:2], t[:, 2:4], ALU.add))
            if g == 0:
                ve_inc(lambda e, t=e4: e.tensor_tensor(ctx4, t[:, 0:1], t[:, 1:2], ALU.add))
            else:
                em(VEC, lambda e, t=e4: e.tensor_tensor(t[:, 0:1], t[:, 0:1], t[:, 1:2], ALU.add))
                ve_inc(lambda e, f=ebflat: e.tensor_tensor(ctx_f[:], ctx_f[:], f[:, 0:2048], ALU.add))

        if nb >= 3:
            emit_product(0)
            emit_product(1)
            yield 5.0
            emit_midA(0)
            yield 5.0
            for g in range(1, ncg):
                if g + 1 < ncg:
                    emit_product(g + 1)
                    yield 4.5
                emit_midB(g - 1)
                yield 6.0
                emit_midA(g)
                yield 5.0
            emit_midB(ncg - 1)
            yield 6.0
        else:
            emit_product(0)
            emit_product(1)
            yield 4.5
            for g in range(ncg):
                emit_midA(g)
                yield 3.0
                emit_midB(g)
                if g + 2 < ncg:
                    emit_product(g + 2)
                yield 5.5

        if KATT < 6:
            return
        ctx_bf = LA["ctx_bf"]
        cvv = ve_inc(lambda e: e.tensor_copy(ctx_bf[:], ctx_f[:]))
        ctxT = LA["ctxT"]
        ev_ctxT = transpose16(ctx_bf, mk_dst_plain(ctxT), first_wait=(s_ve, cvv))
        yield 3.0

        attnout = LA["attnout"]
        op_evacs = []
        for mg in range(4):
            bank = mg % 2
            wait(PE, psm_state[bank][0], psm_state[bank][1])
            if mg == 0:
                wait(PE, s_se, ev_ctxT)
            for kk in range(4):
                m2 = mg * 4 + kk
                dstp = ps_m[:, bank * 512 + kk * 128: bank * 512 + (kk + 1) * 128]
                em(PE, lambda e, d=dstp, mm=m2: e.matmul(
                    d, ctxT[:, mm * 128:(mm + 1) * 128], woutbd[:], start=True, stop=False))
                if kk == 3:
                    pe_inc(lambda e, d=dstp: e.matmul(
                        d, ones_row, bout_row, start=False, stop=True))
                else:
                    em(PE, lambda e, d=dstp: e.matmul(
                        d, ones_row, bout_row, start=False, stop=True))
            wait(SCA, s_pe, s_pe.n)
            evn = se_inc(lambda e, b=bank, m=mg: e.copy(
                attnout[:, m * 512:(m + 1) * 512], ps_m[:, b * 512:(b + 1) * 512]))
            psm_state[bank] = (s_se, evn)
            op_evacs.append(evn)
        yield 2.0

        if KATT < 7:
            return
        wait(VEC, s_se, op_evacs[-1])
        rs2 = emit_ln_stats(attnout[:], 0, tm_scr[:], sca_wait=(s_se, op_evacs[-1]))
        wait(VEC, s_se, rs2)
        em(VEC, lambda e: e.tensor_scalar(ctx_f[:], attnout[:], negm_of(0), rstd_of(0),
                                          ALU.add, ALU.mult))
        r1v = ve_inc(lambda e: e.tensor_tensor(slot, slot, ctx_f[:], ALU.add))
        yield 5.0

        if KATT < 8:
            return
        rs3 = emit_ln_stats(slot, 0, tm_scr[:], sca_wait=(s_ve, r1v))
        wait(VEC, s_se, rs3)
        ln3_done = ve_inc(lambda e: e.tensor_scalar(
            tm_scr[:], slot, negm_of(0), rstd_of(0), ALU.add, ALU.mult))
        yield 4.0

        xfv = xf[h].rearrange("p (f t) -> p f t", f=16)

        def xf_dst(grp):
            def emitfn(e, slot_ap, g=grp, t0=tt):
                return e.copy(xfv[:, g * 4:(g + 1) * 4, t0 * 128:(t0 + 1) * 128],
                              slot_ap.rearrange("p (b f) -> p b f", b=4))
            return emitfn

        xf_ready[h] = transpose16(tm_scr, xf_dst, first_wait=(s_ve, ln3_done))

        if KATT < 9:
            return
        wait(SYNC, s_ve, ln3_done)
        rs_dma[k] = sync_dma(s_rs[k], d_r1[g_tt * 128:(g_tt + 1) * 128, :], slot)
        yield 2.0

    def ffn_half(h, pump):
        xffnT = xf[h]
        srow = ps_u[0:1, 0:512]

        wait(PE, s_se, xf_ready[h])
        wait(PE, s_se, dn_guard[2])
        sx_pe = 0
        for f in range(16):
            rhs = xffnT[:, f * 512:(f + 1) * 512]
            if f == 15:
                sx_pe = pe_inc(lambda e, r=rhs: e.matmul(srow, onesc[:], r, start=False, stop=True))
            else:
                em(PE, lambda e, r=rhs, st=(f == 0): e.matmul(srow, onesc[:], r, start=st, stop=False))
        wait(VEC, s_pe, sx_pe)
        corrx_done = ve_inc(lambda e: e.tensor_scalar(crow[:], srow, -128.0, None, ALU.mult))
        pump(2.0)

        pe_g = [0] * 64
        pe_u = [0] * 64
        hdone = [0] * 64
        for j in range(64):
            gslot = wstage[:, (j % 2) * 2048:(j % 2) * 2048 + 2048]
            uslot = wstage[:, (2 + j % 2) * 2048:(2 + j % 2) * 2048 + 2048]
            if j < 2:
                wait(POOL, s_pe, wslot_guard[j % 2])
            else:
                wait(POOL, s_pe, pe_g[j - 2])
            gv = pool_dma(s_ws[j % 2], gslot, d_wg[j])
            if j < 2:
                wait(POOL, s_pe, wslot_guard[2 + j % 2])
            else:
                wait(POOL, s_pe, pe_u[j - 2])
            uv = pool_dma(s_ws[2 + j % 2], uslot, d_wu[j])

            wait(PE, s_ws[j % 2], gv)
            if j >= 2:
                wait(PE, s_ve, hdone[j - 2])
            else:
                wait(PE, s_se, dn_guard[j])
            gb = ps_g[:, (j % 2) * 512:(j % 2) * 512 + 512]
            ub = ps_u[:, (j % 2) * 512:(j % 2) * 512 + 512]
            for f in range(16):
                lhs = gslot[:, f * 128:(f + 1) * 128]
                rhs = xffnT[:, f * 512:(f + 1) * 512]
                em(PE, lambda e, d=gb, l=lhs, r=rhs, st=(f == 0): e.matmul(d, l, r, start=st, stop=False))
            if j == 0:
                wait(PE, s_ve, corrx_done)
            pe_g[j] = pe_inc(lambda e, d=gb: e.matmul(d, ones_row, crow[:], start=False, stop=True))
            wait(PE, s_ws[2 + j % 2], uv)
            if j < 2:
                wait(PE, s_se, dn_guard[2 + j])
            for f in range(16):
                lhs = uslot[:, f * 128:(f + 1) * 128]
                rhs = xffnT[:, f * 512:(f + 1) * 512]
                em(PE, lambda e, d=ub, l=lhs, r=rhs, st=(f == 0): e.matmul(d, l, r, start=st, stop=False))
            pe_u[j] = pe_inc(lambda e, d=ub: e.matmul(d, ones_row, crow[:], start=False, stop=True))

            sp = (j % 2) * 512
            if j >= 2:
                wait(SCA, s_ve, hdone[j - 2])
            wait(SCA, s_pe, pe_g[j])
            se_inc(lambda e, g=gb, jj=j, s=sp: e.activation(
                scr1[:, s:s + 512], g, AF.Identity, scale=hgsc(jj)))
            se_inc(lambda e, s=sp: e.activation(
                scr2[:, s:s + 512], scr1[:, s:s + 512], AF.Tanh))
            th1_v = se_inc(lambda e, s=sp: e.activation(
                scr2[:, s:s + 512], scr2[:, s:s + 512], AF.Identity, bias=cns[:, 3:4]))
            wait(VEC, s_se, th1_v)
            ve_inc(lambda e, s=sp: e.tensor_tensor(
                scr1[:, s:s + 512], scr1[:, s:s + 512], scr2[:, s:s + 512], ALU.mult))
            vsync()
            wait(VEC, s_pe, pe_u[j])
            hdone[j] = ve_inc(lambda e, u=ub, jj=j, s=sp: e.scalar_tensor_tensor(
                h_sb[:, jj * 512:(jj + 1) * 512], u, usc(jj), scr1[:, s:s + 512], ALU.mult, ALU.mult))
            pump(5.2)

        wait(PE, s_ve, hdone[63])
        sh_pe = 0
        for i in range(64):
            r = h_sb[:, i * 512:(i + 1) * 512]
            if i == 63:
                sh_pe = pe_inc(lambda e, rr=r: e.matmul(srow, onesc[:], rr, start=False, stop=True))
            else:
                em(PE, lambda e, rr=r, st=(i == 0): e.matmul(srow, onesc[:], rr, start=st, stop=False))
        wait(VEC, s_pe, sh_pe)
        corrh_done = ve_inc(lambda e: e.tensor_scalar(crow_f[:], srow, -128.0, None, ALU.mult))

        NQ = 64
        dn_dv = [0] * NQ
        pe_q = [0] * NQ
        dvac = [0] * 16
        pe_dn = [0] * 16
        dTv = xffnT.rearrange("p (t f) -> p t f", t=NTT)

        def emit_down_dma(q):
            hhq, qq = q // 4, q % 4
            slot = wstage[:, (q % 4) * 2048:(q % 4) * 2048 + 2048]
            if q < 4:
                first_guards = [pe_g[62], pe_g[63], pe_u[62], pe_u[63]]
                wait(POOL, s_pe, first_guards[q])
            else:
                wait(POOL, s_pe, pe_q[q - 4])
            dn_dv[q] = pool_dma(s_ws[q % 4], slot, d_wd[hhq][:, qq * 2048:(qq + 1) * 2048])

        def emit_dn_transpose(hh):
            dslot = dsb[:, (hh % 2) * 512:(hh % 2) * 512 + 512]

            def emitfn(e, slot_ap, j2=hh):
                return e.copy(dTv[:, :, j2 * 128:(j2 + 1) * 128],
                              slot_ap.rearrange("p (b f) -> p b f", b=4))

            srcs = [dslot[:, b * 128:(b + 1) * 128] for b in range(4)]
            emit_transposes([(srcs, emitfn)], first_wait=(s_se, dvac[hh]))

        emit_down_dma(0)
        emit_down_dma(1)
        dn_banks = [ps_g[:, 0:512], ps_g[:, 512:1024], ps_u[:, 0:512], ps_u[:, 512:1024]]
        for hh in range(16):
            db = dn_banks[hh % 4]
            if hh >= 4:
                wait(PE, s_se, dvac[hh - 4])
            if hh == 0:
                wait(PE, s_ve, corrh_done)
            for qq in range(4):
                q = hh * 4 + qq
                if q + 2 < NQ:
                    emit_down_dma(q + 2)
                wait(PE, s_ws[q % 4], dn_dv[q])
                for i2 in range(16):
                    i = qq * 16 + i2
                    lhs = wstage[:, (q % 4) * 2048 + i2 * 128: (q % 4) * 2048 + (i2 + 1) * 128]
                    rhs = h_sb[:, i * 512:(i + 1) * 512]
                    if i2 == 15:
                        pe_q[q] = pe_inc(lambda e, d=db, l=lhs, r=rhs: e.matmul(
                            d, l, r, start=False, stop=False))
                    else:
                        em(PE, lambda e, d=db, l=lhs, r=rhs, st=(i == 0): e.matmul(d, l, r, start=st, stop=False))
            pe_dn[hh] = pe_inc(lambda e, d=db: e.matmul(
                d, ones_fr[:], crow_f[:], start=False, stop=True))
            wait(SCA, s_pe, pe_dn[hh])
            dvac[hh] = se_inc(lambda e, d=dsb[:, (hh % 2) * 512:(hh % 2) * 512 + 512],
                              s=db, jj=hh: e.activation(d, s, AF.Identity, scale=dsc(jj)))
            if hh >= 1:
                emit_dn_transpose(hh - 1)
            pump(24.0)
        emit_dn_transpose(15)
        pe_after_down[0] = s_pe.n
        for b in range(4):
            dn_guard[b] = dvac[12 + b]
            wslot_guard[b] = pe_q[60 + b]

    def emit_ln4(h):
        last_dn = max(tr_state)
        dump1 = tm_scr[:]
        wait(VEC, s_se, last_dn)
        for tt in range(NTT):
            g_tt = h * NTT + tt
            k = rnext[0] % 2
            rnext[0] += 1
            slot = rsl[k]
            wait(SYNC, s_rs[k], rs_dma[k])
            rbv = sync_dma(s_rs[k], slot, d_r1[g_tt * 128:(g_tt + 1) * 128, :])
            rs_dma[k] = rbv
            dT_tt = xf[h][:, tt * 2048:(tt + 1) * 2048]
            rs4 = emit_ln_stats(dT_tt, 1, dump1, sca_wait=(s_se, last_dn) if tt == 0 else None)
            yield 12.0
            wait(VEC, s_se, rs4)
            em(VEC, lambda e, d=dT_tt: e.tensor_scalar(ctx_f[:], d, negm_of(1), rstd_of(1),
                                                       ALU.add, ALU.mult))
            wait(VEC, s_rs[k], rbv)
            r2v = ve_inc(lambda e, r=slot: e.tensor_tensor(r, r, ctx_f[:], ALU.add))
            wait(SYNC, s_ve, r2v)
            rs_dma[k] = sync_dma(s_rs[k], d_y[g_tt * 128:(g_tt + 1) * 128, :], slot)
            yield 12.0

    LAYA = dict(cg=4, ebs=[ebA0, ebA1, ebA2], qkv=qkvA, zscr=zscrA,
                ln1T=jit1A, ctx_bf=jit1A, ctxT=jit2A, attnout=jit3A)
    LAYB = dict(cg=2, ebs=[ebB0, ebB1], qkv=qkvB, zscr=zscrB,
                ln1T=jit1B, ctx_bf=jit1B, ctxT=ctxTB, attnout=attnoutB)

    def make_pump(it):
        state = {"alive": True, "acc": 0.0}

        def pump(budget):
            state["acc"] += budget
            while state["alive"] and state["acc"] > 0.0:
                try:
                    state["acc"] -= next(it)
                except StopIteration:
                    state["alive"] = False

        def drain():
            while state["alive"]:
                try:
                    next(it)
                except StopIteration:
                    state["alive"] = False
        return pump, drain

    for tt in range(NTT):
        for _ in attn_tile(0, tt, LAYA):
            pass

    def gen_b():
        for tt in range(NTT):
            yield from attn_tile(1, tt, LAYB)

    pump_b, drain_b = make_pump(gen_b())
    ffn_half(0, pump_b)
    drain_b()
    pump_4, drain_4 = make_pump(emit_ln4(0))
    ffn_half(1, pump_4)
    drain_4()
    for _ in emit_ln4(1):
        pass

    for k in range(2):
        wait(SYNC, s_rs[k], rs_dma[k])

    with nc.Block() as block:
        @block.sync
        def _(eng):
            for fn in prog[SYNC]:
                fn(eng)

        @block.tensor
        def _(eng):
            for fn in prog[PE]:
                fn(eng)

        @block.vector
        def _(eng):
            for fn in prog[VEC]:
                fn(eng)

        @block.scalar
        def _(eng):
            for fn in prog[SCA]:
                fn(eng)

        @block.gpsimd
        def _(eng):
            for fn in prog[POOL]:
                fn(eng)

    return nc



def _prep_weights(Wqkv, bqkv, Wout, bout, gate_w, gate_scale, up_w, up_scale,
                  down_w, down_scale):
    f32 = np.float32
    wqkv = np.asarray(Wqkv, dtype=f32)
    wqkv_p = np.ascontiguousarray(
        wqkv.reshape(16, 2, 64, 192).transpose(1, 2, 0, 3)).reshape(128, 3072)

    rows = np.zeros((1, 6400), dtype=f32)
    rows[0, 0:6144] = np.asarray(bqkv, dtype=f32).reshape(32, 192).reshape(-1)
    bo = np.asarray(bout, dtype=f32).reshape(-1)
    rows[0, 6144:6208] = bo
    rows[0, 6208:6272] = bo
    rows[0, 6272:6400] = 1.0

    wo = np.asarray(Wout, dtype=f32)
    wout_bd = np.zeros((128, 128), dtype=f32)
    wout_bd[0:64, 0:64] = wo
    wout_bd[64:128, 64:128] = wo

    ident = np.eye(128, dtype=f32).astype(ml_dtypes.bfloat16)

    scales = np.zeros((128, 144), dtype=f32)
    scales[:, 0:64] = np.asarray(gate_scale, dtype=f32).reshape(64, 128).T
    scales[:, 64:128] = np.asarray(up_scale, dtype=f32).reshape(64, 128).T
    scales[:, 128:144] = np.asarray(down_scale, dtype=f32).reshape(16, 128).T

    gw = np.asarray(gate_w, dtype=np.int32)
    uw = np.asarray(up_w, dtype=np.int32)
    dw = np.asarray(down_w, dtype=np.int32)
    wg_s = np.ascontiguousarray(gw.reshape(64, 128, 16, 128).transpose(0, 3, 2, 1)).reshape(64, 128, 2048)
    wu_s = np.ascontiguousarray(uw.reshape(64, 128, 16, 128).transpose(0, 3, 2, 1)).reshape(64, 128, 2048)
    wd_s = np.ascontiguousarray(dw.reshape(16, 128, 64, 128).transpose(0, 3, 2, 1)).reshape(16, 128, 8192)

    return dict(wqkv_p=wqkv_p, rows_f32=rows, wout_bd=wout_bd, identity=ident,
                scales=scales, wg_s=wg_s, wu_s=wu_s, wd_s=wd_s)


def kernel(L0_attn_x, L0_attn_Wqkv, L0_attn_bqkv, L0_attn_Wout, L0_attn_bout,
           L0_ffn_gate_w, L0_ffn_gate_scale, L0_ffn_up_w, L0_ffn_up_scale,
           L0_ffn_down_w, L0_ffn_down_scale):
    x = np.ascontiguousarray(np.asarray(L0_attn_x, dtype=np.float32).reshape(8192, 2048))
    shared = _prep_weights(L0_attn_Wqkv, L0_attn_bqkv, L0_attn_Wout, L0_attn_bout,
                           L0_ffn_gate_w, L0_ffn_gate_scale, L0_ffn_up_w, L0_ffn_up_scale,
                           L0_ffn_down_w, L0_ffn_down_scale)
    nc = build_program()
    in_maps = []
    for c in range(NCORES):
        m = dict(shared)
        m["x"] = np.ascontiguousarray(x[c * TOK:(c + 1) * TOK])
        in_maps.append(m)
    res = run_bass_kernel_spmd(nc, in_maps, core_ids=list(range(NCORES)))
    import os as _os1
    if _os1.environ.get("KR1"):
        globals()["_dbg_r1"] = [r.get("r1_scratch") for r in res.results]
    y = np.concatenate([r["y"] for r in res.results], axis=0)
    return np.ascontiguousarray(y.reshape(1, 8192, 2048).astype(np.float32))
